# version 01
# baseline (speedup 1.0000x reference)
import os
import numpy as np
import ml_dtypes
_BF = ml_dtypes.bfloat16
from contextlib import ExitStack

import concourse.bass as bass
import concourse.mybir as mybir
from concourse.ap import AP
from concourse.bass_utils import run_bass_kernel_spmd

F32 = mybir.dt.float32
BF16 = mybir.dt.bfloat16
AF = mybir.ActivationFunctionType
ALU = mybir.AluOpType
AX = mybir.AxisListType

NCORES = 8
B, L, D = 2, 8192, 4096
DIM, DFF, DEPTH = 512, 2048, 4
TPC = (B * L) // NCORES
NT = TPC // 128
CPC = 64
CHPC = DIM // NCORES
HD, NH = 1024, 4
EPS_LN = 1e-5
SIG_SCALE = 4.0 / 32.0


class Sem:
    def __init__(self, nc, name):
        self.h = nc.alloc_semaphore(name)
        self.n = 0

    def inc(self, instr, k=1):
        instr.then_inc(self.h, k)
        self.n += k
        return self.n

    def d(self, instr):
        return self.inc(instr, 16)

    def wait(self, eng, val=None):
        eng.wait_ge(self.h, self.n if val is None else val)


class Ctx:
    def __init__(self, nc):
        self.nc = nc
        self.sy, self.gp = nc.sync, nc.gpsimd
        self.ve, self.sc, self.te = nc.vector, nc.scalar, nc.tensor
        self.tv = Sem(nc, "tv")
        self.vv = Sem(nc, "vv")
        self.sv = Sem(nc, "sv")
        self.ld = Sem(nc, "ld")
        self.st = Sem(nc, "st")
        self.lw = Sem(nc, "lw")

    def fv(self, instr):
        self.vv.inc(instr)
        self.vv.wait(self.ve)

    def fs(self, instr):
        self.sv.inc(instr)
        self.sv.wait(self.sc)

    def join(self, engines):
        for eng in engines:
            if eng is not self.te:
                self.tv.wait(eng)
            if eng is not self.ve:
                self.vv.wait(eng)
            if eng is not self.sc:
                self.sv.wait(eng)


def w_chunks_view(Wt, rows, cols):
    return AP(Wt.tensor, Wt.offset, [[cols, 128], [128 * cols, rows // 128], [1, cols]])


def bcast_row(dram_row, n, col0=0):
    return AP(dram_row.tensor, dram_row.offset + col0, [[0, 128], [1, n]])


def emit_ln(c, out, x, n, st8, scr, wrep=None, brep=None, eps_ap=None):
    ve, sc = c.ve, c.sc
    ve.tensor_reduce(st8[:, 0:1], x, AX.X, ALU.add)
    c.vv.inc(ve.tensor_reduce(st8[:, 6:7], x[:, 0:2], AX.X, ALU.add))
    c.vv.wait(sc)
    c.sv.inc(sc.mul(st8[:, 1:2], st8[:, 0:1], 1.0 / n))
    c.sv.wait(ve)
    ve.tensor_scalar_sub(scr, x, st8[:, 1:2])
    ve.tensor_mul(out, scr, scr)
    ve.tensor_reduce(st8[:, 2:3], out, AX.X, ALU.add)
    c.vv.inc(ve.tensor_reduce(st8[:, 6:7], out[:, 0:2], AX.X, ALU.add))
    c.vv.wait(sc)
    c.fs(sc.activation(st8[:, 3:4], st8[:, 2:3], AF.Ln, bias=eps_ap, scale=1.0 / n))
    c.sv.inc(sc.activation(st8[:, 4:5], st8[:, 3:4], AF.Exp, scale=-0.5))
    c.sv.wait(ve)
    if wrep is not None:
        ve.tensor_scalar_mul(scr, scr, st8[:, 4:5])
        ve.tensor_mul(out, scr, wrep)
        c.vv.inc(ve.tensor_add(out, out, brep))
    else:
        c.vv.inc(ve.tensor_scalar_mul(out, scr, st8[:, 4:5]))


def emit_silu(c, out, x, scr):
    c.sv.inc(c.sc.activation(scr, x, AF.Tanh, scale=0.5))
    c.sv.wait(c.ve)
    c.ve.tensor_scalar(scr, scr, 0.5, 0.5, ALU.mult, ALU.add)
    c.vv.inc(c.ve.tensor_mul(out, scr, x))


def complex_mul_cols(c, outr, outi, ar, ai, brs, bis, t1, t2):
    ve = c.ve
    ve.tensor_scalar(t1, ar, brs, None, ALU.mult)
    c.fv(ve.tensor_scalar(t2, ai, bis, None, ALU.mult))
    c.fv(ve.tensor_sub(outr, t1, t2))
    ve.tensor_scalar(t1, ar, bis, None, ALU.mult)
    c.fv(ve.tensor_scalar(t2, ai, brs, None, ALU.mult))
    c.fv(ve.tensor_add(outi, t1, t2))


def build_winA0():
    nc = bass.Bass("TRN2", target_bir_lowering=False,
                   detect_race_conditions=os.environ.get("KDET") == "1")
    xT = nc.dram_tensor("xT", [D, TPC], F32, kind="ExternalInput").ap()
    WinT = nc.dram_tensor("WinT", [D, DIM], F32, kind="ExternalInput").ap()
    bin_r = nc.dram_tensor("bin_r", [1, DIM], F32, kind="ExternalInput").ap()
    l1w_r = nc.dram_tensor("l1w_r", [1, DIM], F32, kind="ExternalInput").ap()
    l1b_r = nc.dram_tensor("l1b_r", [1, DIM], F32, kind="ExternalInput").ap()
    h_o = nc.dram_tensor("h", [TPC, DIM], F32, kind="ExternalOutput").ap()
    y_o = nc.dram_tensor("y", [TPC, DIM], F32, kind="ExternalOutput").ap()

    c = Ctx(nc)
    sy, gp, ve, sc, te = c.sy, c.gp, c.ve, c.sc, c.te

    with ExitStack() as ex:
        sb = lambda n_, s_, dt=F32: ex.enter_context(nc.sbuf_tensor(n_, s_, dt))
        W_bf = sb("W_bf", [128, 32 * DIM], BF16)
        binr = sb("binr", [128, DIM])
        l1wr = sb("l1wr", [128, DIM])
        l1br = sb("l1br", [128, DIM])
        xbf = [sb(f"xbf{i}", [128, 32 * 128], BF16) for i in range(2)]
        hrow = [sb(f"hrow{i}", [128, DIM]) for i in range(2)]
        yrow = [sb(f"yrow{i}", [128, DIM]) for i in range(2)]
        scr = [sb(f"scrA{i}", [128, DIM]) for i in range(2)]
        stats = sb("stats", [128, 16])
        epsc = sb("epsc", [128, 1])
        ph = [ex.enter_context(nc.psum_tensor(f"ph{i}", [128, DIM], F32)) for i in range(2)]

        c.lw.d(gp.dma_start(W_bf.ap().rearrange("p (k n) -> p k n", n=DIM),
                            w_chunks_view(WinT, D, DIM)))
        c.ld.d(sy.dma_start(binr[:], bcast_row(bin_r, DIM)))
        c.ld.d(sy.dma_start(l1wr[:], bcast_row(l1w_r, DIM)))
        c.ld.d(sy.dma_start(l1br[:], bcast_row(l1b_r, DIM)))
        ve.memset(epsc[:], EPS_LN)

        vend = []
        for t in range(NT):
            buf, ps = xbf[t % 2], ph[t % 2]
            hr, yr, sr = hrow[t % 2], yrow[t % 2], scr[t % 2]
            if t == 0:
                c.ld.wait(gp, 16)
            else:
                c.tv.wait(gp, t)
            c.lw.d(gp.dma_start(buf.ap().rearrange("p (k n) -> p k n", n=128),
                                AP(xT.tensor, xT.offset + t * 128,
                                   [[TPC, 128], [128 * TPC, 32], [1, 128]])))
            c.ld.wait(te)
            c.lw.wait(te)
            if t >= 2:
                c.vv.wait(te, vend[t - 2])
                c.st.wait(ve)
            for k in range(32):
                mm = te.matmul(ps[:], buf[:, k * 128:(k + 1) * 128],
                               W_bf[:, k * DIM:(k + 1) * DIM],
                               start=(k == 0), stop=(k == 31))
            c.tv.inc(mm)
            c.tv.wait(ve)
            ve.tensor_add(hr[:], ps[:], binr[:])
            emit_ln(c, yr[:], hr[:], DIM,
                    stats[:, 8 * (t % 2):8 * (t % 2) + 8], sr[:], l1wr[:], l1br[:],
                    eps_ap=epsc[:, 0:1])
            vend.append(c.vv.n)
            c.vv.wait(sc)
            emit_silu(c, yr[:], yr[:], sr[:])
            c.vv.wait(sy)
            c.st.d(sy.dma_start(h_o[t * 128:(t + 1) * 128, :], hr[:]))
            c.st.d(sy.dma_start(y_o[t * 128:(t + 1) * 128, :], yr[:]))
        c.st.wait(sy)
    return nc


def build_sconv():
    nc = bass.Bass("TRN2", target_bir_lowering=False,
                   detect_race_conditions=os.environ.get("KDET") == "1")
    yT = nc.dram_tensor("yT", [128, CHPC * 2 * CPC], F32, kind="ExternalInput").ap()
    phr_i = nc.dram_tensor("phr", [CHPC, 1], F32, kind="ExternalInput").ap()
    phi_i = nc.dram_tensor("phi", [CHPC, 1], F32, kind="ExternalInput").ap()
    lcr_i = nc.dram_tensor("lcr", [CHPC, 1], F32, kind="ExternalInput").ap()
    lci_i = nc.dram_tensor("lci", [CHPC, 1], F32, kind="ExternalInput").ap()
    id_i = nc.dram_tensor("idf", [128, 128], F32, kind="ExternalInput").ap()
    out = nc.dram_tensor("convT", [128, CHPC * 132], F32, kind="ExternalOutput").ap()

    stag = nc.dram_tensor("stag", [CHPC, 260], F32, kind="Internal").ap()
    iflat = nc.dram_tensor("iflat", [4, CHPC * 128], F32, kind="Internal").ap()

    c = Ctx(nc)
    sy, gp, ve, sc, te = c.sy, c.gp, c.ve, c.sc, c.te

    with ExitStack() as ex:
        sb = lambda n_, s_, dt=F32: ex.enter_context(nc.sbuf_tensor(n_, s_, dt))
        ysb = sb("ysb", [128, CHPC * 2 * CPC])
        Ttab = sb("Ttab", [128, CHPC * 128])
        grev = sb("grev", [128, CHPC])
        conv = sb("conv", [128, CHPC * 132])
        ident = sb("ident", [128, 128])
        G_re = sb("G_re", [CHPC, 132])
        G_im = sb("G_im", [CHPC, 132])
        Gt = sb("Gt", [CHPC, 272])
        pp = sb("pp", [CHPC, 8])
        stg = sb("stg", [CHPC, 260])
        ER = sb("ER", [128, CHPC])
        EI = sb("EI", [128, CHPC])
        S_ = [sb(f"S{i}", [CHPC, 256]) for i in range(2)]
        MP = sb("MP", [CHPC, 16])
        PT = sb("PT", [CHPC, 256])
        I_ = sb("I_", [CHPC, 256])
        ifl = sb("ifl", [2, CHPC * 128])
        ifp = sb("ifp", [2, CHPC * 128])

        psU = [ex.enter_context(nc.psum_tensor(f"pU{i}", [128, 132], F32)) for i in range(6)]
        psT = ex.enter_context(nc.psum_tensor("pT", [128, 128], F32))

        c.ld.d(sy.dma_start(ysb[:], yT))
        c.ld.d(sy.dma_start(ident[:], id_i))
        c.ld.d(sy.dma_start(pp[:, 0:1], phr_i))
        c.ld.d(sy.dma_start(pp[:, 1:2], phi_i))
        c.ld.d(sy.dma_start(pp[:, 2:3], lcr_i))
        c.ld.d(sy.dma_start(pp[:, 3:4], lci_i))
        c.ld.wait(ve)

        ve.tensor_mul(Gt[:, 0:1], pp[:, 0:1], pp[:, 0:1])
        c.fv(ve.tensor_mul(Gt[:, 1:2], pp[:, 1:2], pp[:, 1:2]))
        c.fv(ve.tensor_add(Gt[:, 2:3], Gt[:, 0:1], Gt[:, 1:2]))
        c.vv.inc(ve.tensor_scalar_max(Gt[:, 2:3], Gt[:, 2:3], 1e-38))
        c.vv.wait(sc)
        c.fs(sc.activation(Gt[:, 3:4], Gt[:, 2:3], AF.Ln))
        c.fs(sc.activation(Gt[:, 4:5], Gt[:, 3:4], AF.Exp, scale=0.5))
        c.fs(sc.activation(Gt[:, 5:6], Gt[:, 3:4], AF.Exp, scale=-0.5))
        c.sv.inc(sc.activation(Gt[:, 6:7], Gt[:, 4:5], AF.Tanh))
        c.sv.wait(ve)
        c.fv(ve.tensor_mul(pp[:, 4:5], Gt[:, 6:7], Gt[:, 5:6]))
        c.fv(ve.tensor_mul(pp[:, 0:1], pp[:, 0:1], pp[:, 4:5]))
        c.fv(ve.tensor_mul(pp[:, 1:2], pp[:, 1:2], pp[:, 4:5]))

        ve.memset(G_re[:, 0:1], 1.0)
        ve.memset(G_im[:, 0:1], 0.0)
        ve.tensor_copy(G_re[:, 1:2], pp[:, 0:1])
        c.fv(ve.tensor_copy(G_im[:, 1:2], pp[:, 1:2]))
        n = 2
        while n < 129:
            m = min(n, 129 - n)
            complex_mul_cols(c, Gt[:, 0:1], Gt[:, 1:2],
                             G_re[:, n - 1:n], G_im[:, n - 1:n],
                             pp[:, 0:1], pp[:, 1:2], Gt[:, 4:5], Gt[:, 5:6])
            complex_mul_cols(c, G_re[:, n:n + m], G_im[:, n:n + m],
                             G_re[:, 0:m], G_im[:, 0:m],
                             Gt[:, 0:1], Gt[:, 1:2],
                             Gt[:, 8:8 + m], Gt[:, 136:136 + m])
            n += m

        ve.memset(stg[:], 0.0)
        c.vv.inc(ve.tensor_copy(stg[:, 129:257], G_re[:, 0:128]))
        c.vv.wait(sy)
        c.ld.d(sy.dma_start(stag, stg[:]))
        c.ld.wait(sy)
        for j in range(128):
            src = AP(stag.tensor, stag.offset + 129 - j,
                     [[1, 1], [260, CHPC], [1, 128]])
            dst = AP(Ttab, j * CHPC * 128, [[CHPC * 128, 1], [128, CHPC], [1, 128]])
            c.ld.d(sy.dma_start(dst, src))

        c.vv.inc(ve.tensor_copy(Gt[:, 0:128], AP(G_im, 127, [[132, CHPC], [-1, 128]])))
        c.vv.wait(te)
        c.tv.inc(te.transpose(psT[:, 0:CHPC], Gt[:, 0:128], ident[0:CHPC, 0:CHPC]))
        c.tv.wait(ve)
        c.vv.inc(ve.tensor_copy(grev[:, 0:CHPC], psT[:, 0:CHPC]))

        c.ld.wait(te)
        c.vv.wait(te)
        evd = {}
        for ch in range(CHPC):
            ps = psU[ch % 6]
            if ch >= 6:
                c.vv.wait(te, evd[ch - 6])
            u0, u1 = 2 * ch, 2 * ch + 1
            te.matmul(ps[0:64, 0:128], ysb[:, u0 * 64:(u0 + 1) * 64],
                      Ttab[:, ch * 128:(ch + 1) * 128], start=True, stop=True)
            te.matmul(ps[64:128, 0:128], ysb[:, u1 * 64:(u1 + 1) * 64],
                      Ttab[:, ch * 128:(ch + 1) * 128], start=True, stop=True)
            te.matmul(ps[0:64, 128:129], ysb[:, u0 * 64:(u0 + 1) * 64],
                      grev[:, ch:ch + 1], start=True, stop=True)
            c.tv.inc(te.matmul(ps[64:128, 128:129], ysb[:, u1 * 64:(u1 + 1) * 64],
                               grev[:, ch:ch + 1], start=True, stop=True))
            c.tv.wait(ve)
            evd[ch] = c.vv.inc(ve.tensor_copy(conv[:, ch * 132:ch * 132 + 129],
                                              ps[:, 0:129]))

        ve.tensor_copy(ER[:], AP(conv, 127, [[CHPC * 132, 128], [132, CHPC]]))
        c.vv.inc(ve.tensor_copy(EI[:], AP(conv, 128, [[CHPC * 132, 128], [132, CHPC]])))
        c.vv.wait(te)
        c.tv.inc(te.transpose(psT[0:CHPC, :], ER[:], ident[:]))
        c.tv.wait(ve)
        c.vv.inc(ve.tensor_copy(S_[0][:, 0:128], psT[0:CHPC, :]))
        c.vv.wait(te)
        c.tv.inc(te.transpose(psT[0:CHPC, :], EI[:], ident[:]))
        c.tv.wait(ve)
        c.fv(ve.tensor_copy(S_[0][:, 128:256], psT[0:CHPC, :]))

        ve.tensor_copy(MP[:, 0:1], G_re[:, 128:129])
        c.fv(ve.tensor_copy(MP[:, 1:2], G_im[:, 128:129]))
        for s in range(1, 6):
            a, b2 = 2 * (s - 1), 2 * s
            complex_mul_cols(c, MP[:, b2:b2 + 1], MP[:, b2 + 1:b2 + 2],
                             MP[:, a:a + 1], MP[:, a + 1:a + 2],
                             MP[:, a:a + 1], MP[:, a + 1:a + 2],
                             Gt[:, 0:1], Gt[:, 1:2])

        cur = 0
        for s in range(6):
            h = 1 << s
            nxt = 1 - cur
            A, Bf = S_[cur], S_[nxt]
            mr, mi = MP[:, 2 * s:2 * s + 1], MP[:, 2 * s + 1:2 * s + 2]
            hd2 = [[256, CHPC], [64, 2], [1, h]]
            for base in (0, 128):
                c.fv(ve.tensor_copy(AP(Bf, base, hd2), AP(A, base, hd2)))
            tl = [[64, 2], [1, 64 - h]]
            tRe = AP(Gt, 0, [[272, CHPC], [64, 2], [1, 64 - h]])
            tIm = AP(Gt, 136, [[272, CHPC], [64, 2], [1, 64 - h]])
            aReS = AP(A, 0, [[256, CHPC]] + tl)
            aImS = AP(A, 128, [[256, CHPC]] + tl)
            aRe = AP(A, h, [[256, CHPC]] + tl)
            aIm = AP(A, 128 + h, [[256, CHPC]] + tl)
            bRe = AP(Bf, h, [[256, CHPC]] + tl)
            bIm = AP(Bf, 128 + h, [[256, CHPC]] + tl)
            ve.tensor_scalar(tRe, aReS, mr, None, ALU.mult)
            c.fv(ve.tensor_scalar(tIm, aImS, mi, None, ALU.mult))
            c.fv(ve.tensor_sub(tRe, tRe, tIm))
            c.fv(ve.tensor_add(bRe, aRe, tRe))
            ve.tensor_scalar(tRe, aReS, mi, None, ALU.mult)
            c.fv(ve.tensor_scalar(tIm, aImS, mr, None, ALU.mult))
            c.fv(ve.tensor_add(tIm, tRe, tIm))
            c.fv(ve.tensor_add(bIm, aIm, tIm))
            cur = nxt
        Sf = S_[cur]

        ve.memset(PT[:, 0:1], 1.0)
        c.fv(ve.memset(PT[:, 128:129], 0.0))
        n = 1
        while n < 64:
            m = min(n, 64 - n)
            complex_mul_cols(c, Gt[:, 0:1], Gt[:, 1:2],
                             PT[:, n - 1:n], PT[:, 127 + n:128 + n],
                             MP[:, 0:1], MP[:, 1:2], Gt[:, 4:5], Gt[:, 5:6])
            complex_mul_cols(c, PT[:, n:n + m], PT[:, 128 + n:128 + n + m],
                             PT[:, 0:m], PT[:, 128:128 + m],
                             Gt[:, 0:1], Gt[:, 1:2],
                             Gt[:, 8:8 + m], Gt[:, 136:136 + m])
            n += m

        complex_mul_cols(c, Gt[:, 8:72], Gt[:, 72:136],
                         PT[:, 0:64], PT[:, 128:192],
                         pp[:, 2:3], pp[:, 3:4], Gt[:, 136:200], Gt[:, 200:264])
        for b0 in range(2):
            for half, tcol in ((0, 8), (128, 72)):
                ve.tensor_copy(I_[:, half + 64 * b0:half + 64 * b0 + 1],
                               Gt[:, tcol:tcol + 1])
                c.fv(ve.tensor_add(I_[:, half + 64 * b0 + 1:half + 64 * b0 + 64],
                                   AP(Sf, half + 64 * b0, [[256, CHPC], [1, 63]]),
                                   Gt[:, tcol + 1:tcol + 64]))

        ve.tensor_scalar_mul(Gt[:, 136:264], G_im[:, 1:129], -1.0)
        c.vv.inc(ve.tensor_copy(Gt[:, 8:136], G_re[:, 1:129]))
        c.vv.wait(sy)
        rows = [(I_[:, 0:128], 0), (I_[:, 128:256], 1),
                (Gt[:, 8:136], 2), (Gt[:, 136:264], 3)]
        for src_ap, r in rows:
            dstf = AP(iflat.tensor, iflat.offset + r * CHPC * 128,
                      [[128, CHPC], [1, 128]])
            c.st.d(sy.dma_start(dstf, src_ap))
        c.st.wait(sy)
        c.ld.d(sy.dma_start(ifl[:], AP(iflat.tensor, iflat.offset,
                                       [[CHPC * 128, 2], [1, CHPC * 128]])))
        c.ld.d(sy.dma_start(ifp[:], AP(iflat.tensor, iflat.offset + 2 * CHPC * 128,
                                       [[CHPC * 128, 2], [1, CHPC * 128]])))
        c.ld.wait(te)
        c.vv.wait(te)

        evd2 = {}
        for ch in range(CHPC):
            ps = psU[ch % 6]
            if ch >= 6:
                c.vv.wait(te, evd2[ch - 6])
            c.tv.inc(te.matmul(ps[:, 0:128], ifl[0:2, ch * 128:(ch + 1) * 128],
                               ifp[0:2, ch * 128:(ch + 1) * 128],
                               start=True, stop=True))
            c.tv.wait(ve)
            evd2[ch] = c.vv.inc(
                ve.tensor_add(conv[:, ch * 132:ch * 132 + 128],
                              conv[:, ch * 132:ch * 132 + 128], ps[:, 0:128]))
        c.vv.wait(sy)
        c.st.d(sy.dma_start(out, conv[:]))
        c.st.wait(sy)
    return nc


def build_scanC(last):
    nc = bass.Bass("TRN2", target_bir_lowering=False,
                   detect_race_conditions=os.environ.get("KDET") == "1")
    h_i = nc.dram_tensor("h_i", [TPC, DIM], F32, kind="ExternalInput").ap()
    conv_i = nc.dram_tensor("conv_i", [TPC, DIM], F32, kind="ExternalInput").ap()
    scl_r = nc.dram_tensor("scl_r", [1, DIM], F32, kind="ExternalInput").ap()
    l2w_r = nc.dram_tensor("l2w_r", [1, DIM], F32, kind="ExternalInput").ap()
    l2b_r = nc.dram_tensor("l2b_r", [1, DIM], F32, kind="ExternalInput").ap()
    W1T = nc.dram_tensor("W1T", [DIM, DFF], F32, kind="ExternalInput").ap()
    b1_c = nc.dram_tensor("b1_c", [128, 16], F32, kind="ExternalInput").ap()
    W2T = nc.dram_tensor("W2T", [DFF, DIM], F32, kind="ExternalInput").ap()
    b2_r = nc.dram_tensor("b2_r", [1, DIM], F32, kind="ExternalInput").ap()
    idb_i = nc.dram_tensor("idb", [128, 128], BF16, kind="ExternalInput").ap()
    if last:
        lnlw_r = nc.dram_tensor("lnlw_r", [1, DIM], F32, kind="ExternalInput").ap()
        lnlb_r = nc.dram_tensor("lnlb_r", [1, DIM], F32, kind="ExternalInput").ap()
        WoutT = nc.dram_tensor("WoutT", [DIM, D], F32, kind="ExternalInput").ap()
        bout_r = nc.dram_tensor("bout_r", [1, D], F32, kind="ExternalInput").ap()
        x_i = nc.dram_tensor("x_i", [TPC, D], F32, kind="ExternalInput").ap()
        x1_o = nc.dram_tensor("x1", [TPC, D], F32, kind="ExternalOutput").ap()
    else:
        l1w_r = nc.dram_tensor("l1w_r", [1, DIM], F32, kind="ExternalInput").ap()
        l1b_r = nc.dram_tensor("l1b_r", [1, DIM], F32, kind="ExternalInput").ap()
        h_o = nc.dram_tensor("h_o", [TPC, DIM], F32, kind="ExternalOutput").ap()
        y_o = nc.dram_tensor("y_o", [TPC, DIM], F32, kind="ExternalOutput").ap()

    c = Ctx(nc)
    sy, gp, ve, sc, te = c.sy, c.gp, c.ve, c.sc, c.te

    with ExitStack() as ex:
        sb = lambda n_, s_, dt=F32: ex.enter_context(nc.sbuf_tensor(n_, s_, dt))
        h_sb = sb("h_sb", [128, NT * DIM])
        sclr = sb("sclr", [128, DIM])
        l2wr = sb("l2wr", [128, DIM])
        l2br = sb("l2br", [128, DIM])
        b1c = sb("b1c", [128, 16])
        b2r = sb("b2r", [128, DIM])
        identB = sb("identB", [128, 128], BF16)
        stats = sb("stats", [128, 16])
        epsc = sb("epsc", [128, 1])
        aw = sb("aw", [128, DIM])
        ab = sb("ab", [128, DIM])

        c.ld.d(sy.dma_start(identB[:], idb_i))
        c.ld.d(sy.dma_start(h_sb.ap().rearrange("p (t n) -> p t n", n=DIM),
                            AP(h_i.tensor, h_i.offset,
                               [[DIM, 128], [128 * DIM, NT], [1, DIM]])))
        c.ld.d(sy.dma_start(sclr[:], bcast_row(scl_r, DIM)))
        c.ld.d(sy.dma_start(l2wr[:], bcast_row(l2w_r, DIM)))
        c.ld.d(sy.dma_start(l2br[:], bcast_row(l2b_r, DIM)))
        c.ld.d(sy.dma_start(b1c[:], b1_c))
        c.ld.d(sy.dma_start(b2r[:], bcast_row(b2_r, DIM)))
        if last:
            c.ld.d(sy.dma_start(aw[:], bcast_row(lnlw_r, DIM)))
            c.ld.d(sy.dma_start(ab[:], bcast_row(lnlb_r, DIM)))
        else:
            c.ld.d(sy.dma_start(aw[:], bcast_row(l1w_r, DIM)))
            c.ld.d(sy.dma_start(ab[:], bcast_row(l1b_r, DIM)))
        ve.memset(epsc[:], EPS_LN)

        with ExitStack() as eA:
            sA = lambda n_, s_, dt=F32: eA.enter_context(nc.sbuf_tensor(n_, s_, dt))
            y2T = sA("y2T", [128, 4 * TPC], BF16)
            W1bf = sA("W1bf", [128, 4 * DFF], BF16)
            W2bf = sA("W2bf", [128, 16 * DIM], BF16)
            c.lw.d(gp.dma_start(W1bf.ap().rearrange("p (k n) -> p k n", n=DFF),
                                w_chunks_view(W1T, DIM, DFF)))
            c.lw.d(gp.dma_start(W2bf.ap().rearrange("p (k n) -> p k n", n=DIM),
                                w_chunks_view(W2T, DFF, DIM)))

            lcv = [Sem(nc, "lca"), Sem(nc, "lcb")]
            with ExitStack() as e1:
                s1 = lambda n_, s_, dt=F32: e1.enter_context(nc.sbuf_tensor(n_, s_, dt))
                cbuf = [s1(f"cb{i}", [128, DIM]) for i in range(2)]
                y2 = [s1(f"y2_{i}", [128, DIM]) for i in range(2)]
                y2b = [s1(f"y2b{i}", [128, DIM], BF16) for i in range(2)]
                scr2 = [s1(f"sc2{i}", [128, DIM]) for i in range(2)]
                psB = [e1.enter_context(nc.psum_tensor(f"psB{i}", [128, 128], BF16))
                       for i in range(2)]
                c.ld.wait(ve)
                for t in range(NT):
                    cb, yy, yb, sr = cbuf[t % 2], y2[t % 2], y2b[t % 2], scr2[t % 2]
                    if t >= 2:
                        c.vv.wait(sy)
                    lcv[t % 2].d(sy.dma_start(cb[:], conv_i[t * 128:(t + 1) * 128, :]))
                    lcv[t % 2].wait(ve)
                    ht = h_sb[:, t * DIM:(t + 1) * DIM]
                    ve.tensor_mul(sr[:], cb[:], sclr[:])
                    c.vv.inc(ve.tensor_add(ht, ht, sr[:]))
                    emit_ln(c, yy[:], ht, DIM,
                            stats[:, 8 * (t % 2):8 * (t % 2) + 8], sr[:],
                            l2wr[:], l2br[:], eps_ap=epsc[:, 0:1])
                    c.vv.inc(ve.tensor_copy(yb[:], yy[:]))
                    c.vv.wait(te)
                    for dc in range(4):
                        c.tv.inc(te.transpose(psB[t % 2][:],
                                              yb[:, dc * 128:(dc + 1) * 128],
                                              identB[:]))
                        c.tv.wait(ve)
                        c.vv.inc(ve.tensor_copy(
                            y2T[:, dc * TPC + t * 128:dc * TPC + (t + 1) * 128],
                            psB[t % 2][:]))
                        c.vv.wait(te)

            with ExitStack() as e2:
                s2 = lambda n_, s_, dt=F32: e2.enter_context(nc.sbuf_tensor(n_, s_, dt))
                fT = s2("fT", [128, 16 * TPC], BF16)
                fsc = [s2(f"fsc{i}", [128, 512]) for i in range(2)]
                fzb = [s2(f"fzb{i}", [128, 512]) for i in range(2)]
                psF = [e2.enter_context(nc.psum_tensor(f"psF{i}", [128, 512], F32))
                       for i in range(2)]
                hb1 = s2("hb1", [128, 16])
                c.join([te, ve, sc, sy, gp])
                c.vv.inc(ve.tensor_scalar_mul(hb1[:], b1c[:], 0.5))
                c.lw.wait(te)
                c.vv.wait(te)
                c.vv.wait(sc)
                k = 0
                fdone = {}
                for ft in range(16):
                    for tb in range(4):
                        ps = psF[k % 2]
                        if k >= 2:
                            c.vv.wait(te, fdone[k - 2])
                            c.vv.wait(sc, fdone[k - 2])
                        for dc in range(4):
                            mm = te.matmul(
                                ps[:],
                                W1bf[:, dc * DFF + ft * 128:dc * DFF + (ft + 1) * 128],
                                y2T[:, dc * TPC + tb * 512:dc * TPC + (tb + 1) * 512],
                                start=(dc == 0), stop=(dc == 3))
                        c.tv.inc(mm)
                        c.tv.wait(sc)
                        c.sv.inc(sc.activation(fsc[k % 2][:], ps[:], AF.Tanh,
                                               scale=0.5, bias=hb1[:, ft:ft + 1]))
                        c.tv.wait(ve)
                        c.sv.wait(ve)
                        ve.tensor_scalar_add(fzb[k % 2][:], ps[:], b1c[:, ft:ft + 1])
                        ve.tensor_scalar(fsc[k % 2][:], fsc[k % 2][:], 0.5, 0.5,
                                         ALU.mult, ALU.add)
                        fdone[k] = c.vv.inc(ve.tensor_mul(
                            fT[:, ft * TPC + tb * 512:ft * TPC + (tb + 1) * 512],
                            fsc[k % 2][:], fzb[k % 2][:]))
                        k += 1

                psG = [e2.enter_context(nc.psum_tensor(f"psG{i}", [128, DIM], F32))
                       for i in range(2)]
                gsc = [e2.enter_context(nc.sbuf_tensor(f"gsc{i}", [128, DIM], F32))
                       for i in range(2)]
                gdone = {}
                for t in range(NT):
                    ps = psG[t % 2]
                    if t >= 2:
                        c.vv.wait(te, gdone[t - 2])
                    for fc in range(16):
                        mm = te.matmul(ps[:],
                                       fT[:, fc * TPC + t * 128:fc * TPC + (t + 1) * 128],
                                       W2bf[:, fc * DIM:(fc + 1) * DIM],
                                       start=(fc == 0), stop=(fc == 15))
                    c.tv.inc(mm)
                    c.tv.wait(ve)
                    ht = h_sb[:, t * DIM:(t + 1) * DIM]
                    ve.tensor_add(gsc[t % 2][:], ps[:], b2r[:])
                    gdone[t] = c.vv.inc(ve.tensor_add(ht, ht, gsc[t % 2][:]))

        if not last:
            with ExitStack() as e4:
                s4 = lambda n_, s_, dt=F32: e4.enter_context(nc.sbuf_tensor(n_, s_, dt))
                yy4 = [s4(f"yy4{i}", [128, DIM]) for i in range(2)]
                sc4 = [s4(f"sc4{i}", [128, DIM]) for i in range(2)]
                c.join([te, ve, sc, sy, gp])
                for t in range(NT):
                    yr, sr = yy4[t % 2], sc4[t % 2]
                    if t >= 2:
                        c.st.wait(ve)
                        c.st.wait(sc)
                    ht = h_sb[:, t * DIM:(t + 1) * DIM]
                    c.vv.wait(sc)
                    emit_ln(c, yr[:], ht, DIM,
                            stats[:, 8 * (t % 2):8 * (t % 2) + 8], sr[:],
                            aw[:], ab[:], eps_ap=epsc[:, 0:1])
                    c.vv.wait(sc)
                    emit_silu(c, yr[:], yr[:], sr[:])
                    c.vv.wait(sy)
                    c.st.d(sy.dma_start(h_o[t * 128:(t + 1) * 128, :], ht))
                    c.st.d(sy.dma_start(y_o[t * 128:(t + 1) * 128, :], yr[:]))
            c.st.wait(sy)
        else:
            with ExitStack() as e4:
                s4 = lambda n_, s_, dt=F32: e4.enter_context(nc.sbuf_tensor(n_, s_, dt))
                WoutBF = s4("WoutBF", [128, 4 * D], BF16)
                c.join([te, ve, sc, sy, gp])
                c.lw.d(gp.dma_start(WoutBF.ap().rearrange("p (k n) -> p k n", n=D),
                                    w_chunks_view(WoutT, DIM, D)))
                boutr = s4("boutr", [128, D])
                lxb = Sem(nc, "lxb")
                c.ld.d(sy.dma_start(boutr[:], bcast_row(bout_r, D)))
                hf = s4("hf", [128, DIM])
                hfb = s4("hfb", [128, DIM], BF16)
                hfT = s4("hfT", [128, DIM], BF16)
                srD = s4("srD", [128, DIM])
                xb = s4("xb", [128, D])
                acc = s4("acc", [128, D])
                xr = s4("xr", [128, D])
                scD = s4("scD", [128, D])
                psB2 = e4.enter_context(nc.psum_tensor("psB2", [128, 128], BF16))
                psW = [e4.enter_context(nc.psum_tensor(f"psW{i}", [128, 512], F32))
                       for i in range(2)]
                c.ld.wait(te)
                c.ld.wait(ve)
                c.lw.wait(te)
                wdone = {}
                for t in range(NT):
                    c.st.wait(ve)
                    ht = h_sb[:, t * DIM:(t + 1) * DIM]
                    c.vv.wait(sc)
                    emit_ln(c, hf[:], ht, DIM, stats[:, 0:8], srD[:],
                            aw[:], ab[:], eps_ap=epsc[:, 0:1])
                    c.vv.inc(ve.tensor_copy(hfb[:], hf[:]))
                    c.vv.wait(te)
                    for dc in range(4):
                        c.tv.inc(te.transpose(psB2[:], hfb[:, dc * 128:(dc + 1) * 128],
                                              identB[:]))
                        c.tv.wait(ve)
                        c.vv.inc(ve.tensor_copy(hfT[:, dc * 128:(dc + 1) * 128],
                                                psB2[:]))
                        c.vv.wait(te)
                    c.vv.wait(sy)
                    lxb.d(sy.dma_start(xb[:], x_i[t * 128:(t + 1) * 128, :]))
                    lxb.wait(ve)
                    for ob in range(8):
                        ps = psW[ob % 2]
                        kk = t * 8 + ob
                        if kk >= 2:
                            c.vv.wait(te, wdone[kk - 2])
                        for dc in range(4):
                            mm = te.matmul(ps[:], hfT[:, dc * 128:(dc + 1) * 128],
                                           WoutBF[:, dc * D + ob * 512:dc * D + (ob + 1) * 512],
                                           start=(dc == 0), stop=(dc == 3))
                        c.tv.inc(mm)
                        c.tv.wait(ve)
                        sl = slice(ob * 512, (ob + 1) * 512)
                        ve.tensor_add(acc[:, sl], ps[:], boutr[:, sl])
                        wdone[kk] = c.vv.inc(ve.tensor_add(acc[:, sl], acc[:, sl],
                                                           xb[:, sl]))
                    c.vv.wait(sc)
                    emit_ln(c, xr[:], acc[:], D, stats[:, 0:8], scD[:],
                            None, None, eps_ap=epsc[:, 0:1])
                    c.vv.wait(sy)
                    c.st.d(sy.dma_start(x1_o[t * 128:(t + 1) * 128, :], xr[:]))
            c.st.wait(sy)
    return nc


def build_l0ffn():
    nc = bass.Bass("TRN2", target_bir_lowering=False,
                   detect_race_conditions=os.environ.get("KDET") == "1")
    x1_i = nc.dram_tensor("x1_i", [TPC, D], F32, kind="ExternalInput").ap()
    W0T = nc.dram_tensor("W0T", [D, D], F32, kind="ExternalInput").ap()
    b_r = nc.dram_tensor("b_r", [1, D], F32, kind="ExternalInput").ap()
    Aall = nc.dram_tensor("Aall", [D, 128], F32, kind="ExternalInput").ap()
    Ball = nc.dram_tensor("Ball", [128, D], F32, kind="ExternalInput").ap()
    p_c = nc.dram_tensor("p_c", [128, 1], F32, kind="ExternalInput").ap()
    idb_i = nc.dram_tensor("idb", [128, 128], BF16, kind="ExternalInput").ap()
    x2_o = nc.dram_tensor("x2", [TPC, D], F32, kind="ExternalOutput").ap()

    x1bf = nc.dram_tensor("x1bf", [TPC, D], BF16, kind="Internal").ap()
    zsp = nc.dram_tensor("zsp", [16 * NT, 128 * 256], F32, kind="Internal").ap()

    c = Ctx(nc)
    sy, gp, ve, sc, te = c.sy, c.gp, c.ve, c.sc, c.te

    with ExitStack() as ex:
        sb = lambda n_, s_, dt=F32: ex.enter_context(nc.sbuf_tensor(n_, s_, dt))
        uT = sb("uT", [128, TPC], BF16)
        identB = sb("identB", [128, 128], BF16)
        stats = sb("stats", [128, 16])
        epsc = sb("epsc", [128, 1])
        ep6 = sb("ep6", [128, 1])
        nz2 = sb("nz2", [128, 256])
        nm2 = sb("nm2", [128, 8])
        gcol = sb("gcol", [128, 8])
        pcol = sb("pcol", [128, 1])

        c.ld.d(sy.dma_start(identB[:], idb_i))
        c.ld.d(sy.dma_start(pcol[:], p_c))
        ve.memset(epsc[:], EPS_LN)
        ve.memset(ep6[:], 1e-6)

        exT = ExitStack()
        x1T = exT.enter_context(nc.sbuf_tensor("x1T", [128, 32 * TPC], BF16))
        with ExitStack() as e0:
            s0 = lambda n_, s_, dt=F32: e0.enter_context(nc.sbuf_tensor(n_, s_, dt))
            xtb = [s0(f"xtb{i}", [128, D], BF16) for i in range(2)]
            lxc = [Sem(nc, f"lxc{i}") for i in range(2)]
            for t in range(NT):
                if t == 0:
                    c.ld.wait(gp, 16)
                else:
                    c.st.wait(gp)
                lxc[t % 2].d(gp.dma_start(xtb[t % 2][:], x1_i[t * 128:(t + 1) * 128, :]))
                lxc[t % 2].wait(sy)
                c.st.d(sy.dma_start(x1bf[t * 128:(t + 1) * 128, :], xtb[t % 2][:]))
            c.st.wait(sy)
        ltr = Sem(nc, "ltr")
        for dc in range(32):
            src = AP(x1bf.tensor, x1bf.offset + dc * 128, [[D, TPC], [1, 128]])
            ltr.d(sy.dma_start_transpose(x1T[:, dc * TPC:(dc + 1) * TPC], src))

        with ExitStack() as e1:
            s1 = lambda n_, s_, dt=F32: e1.enter_context(nc.sbuf_tensor(n_, s_, dt))
            Abf = s1("Abf", [128, 32 * 128], BF16)
            ub = s1("ub", [128, 128], BF16)
            psu = e1.enter_context(nc.psum_tensor("psu", [128, 128], F32))
            psb = e1.enter_context(nc.psum_tensor("psb", [128, 128], BF16))
            c.st.wait(gp)
            c.lw.d(gp.dma_start(Abf.ap().rearrange("p (k n) -> p k n", n=128),
                                w_chunks_view(Aall, D, 128)))
            c.lw.wait(te)
            ltr.wait(te)
            for t in range(NT):
                for dc in range(32):
                    mm = te.matmul(psu[:], x1T[:, dc * TPC + t * 128:dc * TPC + (t + 1) * 128],
                                   Abf[:, dc * 128:(dc + 1) * 128],
                                   start=(dc == 0), stop=(dc == 31))
                c.tv.inc(mm)
                c.tv.wait(ve)
                c.vv.inc(ve.tensor_copy(ub[:], psu[:]))
                c.vv.wait(te)
                c.tv.inc(te.transpose(psb[:], ub[:], identB[:]))
                c.tv.wait(ve)
                c.vv.inc(ve.tensor_copy(uT[:, t * 128:(t + 1) * 128], psb[:]))
                c.vv.wait(te)

        with ExitStack() as e2:
            s2 = lambda n_, s_, dt=F32: e2.enter_context(nc.sbuf_tensor(n_, s_, dt))
            Wbuf = [s2(f"Wbuf{i}", [128, 32 * 256], BF16) for i in range(2)]
            zt = [s2(f"zt{i}", [128, 256]) for i in range(2)]
            zq = [s2(f"zq{i}", [128, 256]) for i in range(2)]
            psz = [e2.enter_context(nc.psum_tensor(f"psz{i}", [128, 256], F32))
                   for i in range(2)]
            c.join([te, ve, sc, sy, gp])
            wtv = {}
            zdone = {}
            for ob in range(16):
                if ob >= 2:
                    c.tv.wait(gp, wtv[ob - 2])
                else:
                    c.vv.wait(gp)
                c.lw.d(gp.dma_start(
                    Wbuf[ob % 2].ap().rearrange("p (k n) -> p k n", n=256),
                    AP(W0T.tensor, W0T.offset + ob * 256,
                       [[D, 128], [128 * D, 32], [1, 256]])))
                c.lw.wait(te)
                for t in range(NT):
                    k = ob * NT + t
                    ps = psz[k % 2]
                    if k >= 2:
                        c.vv.wait(te, zdone[k - 2])
                    for dc in range(32):
                        mm = te.matmul(ps[:],
                                       x1T[:, dc * TPC + t * 128:dc * TPC + (t + 1) * 128],
                                       Wbuf[ob % 2][:, dc * 256:(dc + 1) * 256],
                                       start=(dc == 0), stop=(dc == 31))
                    c.tv.inc(mm)
                    if t == NT - 1:
                        wtv[ob] = c.tv.n
                    c.tv.wait(ve)
                    ve.tensor_copy(zt[k % 2][:], ps[:])
                    ve.tensor_mul(zq[k % 2][:], zt[k % 2][:], zt[k % 2][:])
                    ve.tensor_reduce(nz2[:, t * 16 + ob:t * 16 + ob + 1],
                                     zq[k % 2][:], AX.X, ALU.add)
                    zdone[k] = c.vv.inc(ve.tensor_reduce(stats[:, 15:16],
                                                         zq[k % 2][:, 0:2], AX.X, ALU.add))
                    c.vv.wait(sy)
                    c.st.d(sy.dma_start(zsp[ob * NT + t, :], zt[k % 2][:]))
            c.st.wait(sy)

        exT.close()
        with ExitStack() as e3:
            s3 = lambda n_, s_, dt=F32: e3.enter_context(nc.sbuf_tensor(n_, s_, dt))
            Bbf = s3("Bbf", [128, D], BF16)
            br = s3("br", [128, D])
            x1t = s3("x1t", [128, D])
            acc = s3("acc", [128, D])
            xr = s3("xr", [128, D])
            scD = s3("scD", [128, D])
            mq = s3("mq", [128, 512])
            zrd = s3("zrd", [128, D])
            psm = [e3.enter_context(nc.psum_tensor(f"psm{i}", [128, 512], F32))
                   for i in range(8)]
            c.join([te, ve, sc, sy, gp])
            c.vv.wait(gp)
            c.lw.d(gp.dma_start(Bbf[:], Ball))
            c.lw.wait(ve)
            c.ld.wait(ve)
            c.vv.inc(ve.tensor_scalar(Bbf[:], Bbf[:], pcol[:, 0:1], None, ALU.mult))
            c.ld.d(sy.dma_start(br[:], bcast_row(b_r, D)))
            lz = Sem(nc, "lz")
            lx1 = Sem(nc, "lx1")
            c.vv.wait(te)
            for t in range(NT):
                if t > 0:
                    c.vv.wait(sy)
                lx1.d(sy.dma_start(x1t[:], x1_i[t * 128:(t + 1) * 128, :]))
                lz.d(sy.dma_start(zrd.ap().rearrange("p (a n) -> p a n", n=256),
                                  AP(zsp.tensor, zsp.offset + t * 128 * 256,
                                     [[256, 128], [NT * 128 * 256, 16], [1, 256]])))
                if t > 0:
                    c.vv.wait(te)
                for ob8 in range(8):
                    mm = te.matmul(psm[ob8][:], uT[:, t * 128:(t + 1) * 128],
                                   Bbf[:, ob8 * 512:(ob8 + 1) * 512],
                                   start=True, stop=True)
                c.tv.inc(mm)
                c.tv.wait(ve)
                for ob8 in range(8):
                    ve.tensor_copy(mq[:], psm[ob8][:])
                    ve.tensor_mul(mq[:], mq[:], mq[:])
                    ve.tensor_reduce(nm2[:, ob8:ob8 + 1], mq[:], AX.X, ALU.add)
                c.vv.inc(ve.tensor_reduce(stats[:, 15:16], mq[:, 0:2], AX.X, ALU.add))
                ve.tensor_reduce(stats[:, 0:1], nz2[:, t * 16:t * 16 + 16], AX.X, ALU.add)
                ve.tensor_reduce(stats[:, 8:9], nz2[:, 0:2], AX.X, ALU.add)
                ve.tensor_reduce(stats[:, 1:2], nm2[:], AX.X, ALU.add)
                c.vv.inc(ve.tensor_reduce(stats[:, 8:9], nm2[:, 0:2], AX.X, ALU.add))
                c.vv.wait(sc)
                c.fs(sc.activation(stats[:, 2:3], stats[:, 1:2], AF.Ln))
                c.fs(sc.activation(stats[:, 3:4], stats[:, 2:3], AF.Exp, scale=0.5))
                c.fs(sc.activation(stats[:, 4:5], stats[:, 3:4], AF.Ln, bias=ep6[:, 0:1]))
                c.sv.inc(sc.activation(stats[:, 5:6], stats[:, 0:1], AF.Ln))
                c.sv.wait(ve)
                c.fv(ve.scalar_tensor_tensor(stats[:, 6:7], stats[:, 5:6], 0.5,
                                             stats[:, 4:5], ALU.mult, ALU.subtract))
                c.vv.wait(sc)
                c.sv.inc(sc.activation(stats[:, 7:8], stats[:, 6:7], AF.Exp))
                c.sv.wait(ve)
                c.fv(ve.tensor_scalar(gcol[:, 0:1], stats[:, 7:8], 0.5, 1.0,
                                      ALU.mult, ALU.min))
                lz.wait(ve)
                lx1.wait(ve)
                for ob8 in range(8):
                    sl = slice(ob8 * 512, (ob8 + 1) * 512)
                    ve.scalar_tensor_tensor(acc[:, sl], psm[ob8][:], gcol[:, 0:1],
                                            zrd[:, sl], ALU.mult, ALU.add)
                ve.tensor_add(acc[:], acc[:], x1t[:])
                c.vv.inc(ve.tensor_add(acc[:], acc[:], br[:]))
                c.vv.wait(sc)
                emit_ln(c, xr[:], acc[:], D, stats[:, 0:8], scD[:],
                        None, None, eps_ap=epsc[:, 0:1])
                c.vv.wait(sy)
                c.st.d(sy.dma_start(x2_o[t * 128:(t + 1) * 128, :], xr[:]))
            c.st.wait(sy)
    return nc


def build_attn():
    nc = bass.Bass("TRN2", target_bir_lowering=False,
                   detect_race_conditions=os.environ.get("KDET") == "1")
    xh = nc.dram_tensor("xh", [L, HD], F32, kind="ExternalInput").ap()
    xhT = nc.dram_tensor("xhT", [HD, L], F32, kind="ExternalInput").ap()
    bq_r = nc.dram_tensor("bq_r", [1, HD], F32, kind="ExternalInput").ap()
    bk_r = nc.dram_tensor("bk_r", [1, HD], F32, kind="ExternalInput").ap()
    bv_r = nc.dram_tensor("bv_r", [1, HD], F32, kind="ExternalInput").ap()
    m0_i = nc.dram_tensor("m0", [128, 256], BF16, kind="ExternalInput").ap()
    m1_i = nc.dram_tensor("m1", [128, 256], BF16, kind="ExternalInput").ap()
    id_i = nc.dram_tensor("idf", [128, 128], F32, kind="ExternalInput").ap()
    ao_o = nc.dram_tensor("ao", [L, HD], F32, kind="ExternalOutput").ap()

    vbf = nc.dram_tensor("vbf", [L, HD], BF16, kind="Internal").ap()
    vbst = nc.dram_tensor("vbst", [1, HD], F32, kind="Internal").ap()

    c = Ctx(nc)
    sy, gp, ve, sc, te = c.sy, c.gp, c.ve, c.sc, c.te
    NQ = L // 256

    with ExitStack() as ex:
        sb = lambda n_, s_, dt=F32: ex.enter_context(nc.sbuf_tensor(n_, s_, dt))
        xT = sb("xT", [128, 8 * L], BF16)
        vbrep = sb("vbrep", [128, HD], BF16)
        m0 = sb("m0s", [128, 256], BF16)
        m1 = sb("m1s", [128, 256], BF16)
        ident = sb("ident", [128, 128])
        qkc = sb("qkc", [128, 8])
        exB = ExitStack()
        bvs = exB.enter_context(nc.sbuf_tensor("bvs", [1, 3 * HD], F32))
        bw = exB.enter_context(nc.sbuf_tensor("bw", [1, 4 * HD], F32))
        al = exB.enter_context(nc.sbuf_tensor("al", [1, 8], F32))
        psS = [ex.enter_context(nc.psum_tensor(f"psS{i}", [128, 256], F32))
               for i in range(2)]
        psO = ex.enter_context(nc.psum_tensor("psO", [128, 2048], F32))
        psq = ex.enter_context(nc.psum_tensor("psq", [128, 8], F32))

        c.ld.d(sy.dma_start(bvs[:, 0:HD], bq_r))
        c.ld.d(sy.dma_start(bvs[:, HD:2 * HD], bk_r))
        c.ld.d(sy.dma_start(bvs[:, 2 * HD:3 * HD], bv_r))
        c.ld.d(sy.dma_start(m0[:], m0_i))
        c.ld.d(sy.dma_start(m1[:], m1_i))
        c.ld.d(sy.dma_start(ident[:], id_i))
        c.ld.wait(gp)
        c.lw.d(gp.dma_start(xT.ap().rearrange("p (k n) -> p k n", n=L),
                            w_chunks_view(xhT, HD, L)))

        c.ld.wait(ve)
        c.ld.wait(sc)
        for i in range(3):
            seg = bvs[:, i * HD:(i + 1) * HD]
            hseg = bw[:, i * HD:(i + 1) * HD]
            tmp = bw[:, 3 * HD:4 * HD]
            ve.tensor_reduce(al[:, i:i + 1], seg, AX.X, ALU.add,
                             apply_absolute_value=True)
            c.fv(ve.tensor_reduce(al[:, 7:8], seg[:, 0:2], AX.X, ALU.add))
            c.fv(ve.tensor_scalar_mul(al[:, i:i + 1], al[:, i:i + 1], 1.0 / HD))
            c.vv.wait(sc)
            c.sv.inc(sc.sign(tmp, seg))
            c.sv.wait(ve)
            c.fv(ve.tensor_scalar_mul(tmp, tmp, al[:, i:i + 1]))
            c.fv(ve.tensor_scalar(seg, seg, 0.0, None, ALU.is_equal))
            c.fv(ve.tensor_scalar_mul(seg, seg, al[:, i:i + 1]))
            c.fv(ve.tensor_add(hseg, tmp, seg))
        c.fv(ve.tensor_mul(bw[:, 3 * HD:4 * HD], bw[:, 0:HD], bw[:, HD:2 * HD]))
        c.vv.wait(sy)
        c.st.d(sy.dma_start(vbst, bw[:, 2 * HD:3 * HD]))
        c.st.wait(gp)
        lvb = Sem(nc, "lvb")
        lvb.d(gp.dma_start(vbrep[:], bcast_row(vbst, HD)))
        lvb.wait(ve)
        c.vv.wait(te)
        for cc in range(8):
            c.tv.inc(te.transpose(psq[:, cc:cc + 1],
                                  bw[0:1, 3 * HD + cc * 128:3 * HD + (cc + 1) * 128],
                                  ident[0:1, 0:1]))
        c.tv.wait(ve)
        c.fv(ve.tensor_copy(qkc[:], psq[:]))

        with ExitStack() as e0:
            s0 = lambda n_, s_, dt=F32: e0.enter_context(nc.sbuf_tensor(n_, s_, dt))
            vtb = [s0(f"vtb{i}", [128, HD], BF16) for i in range(2)]
            lvc = [Sem(nc, f"lvc{i}") for i in range(2)]
            for t in range(64):
                if t == 0:
                    c.ld.wait(gp)
                else:
                    c.st.wait(gp)
                lvc[t % 2].d(gp.dma_start(vtb[t % 2][:], xh[t * 128:(t + 1) * 128, :]))
                lvc[t % 2].wait(sy)
                c.st.d(sy.dma_start(vbf[t * 128:(t + 1) * 128, :], vtb[t % 2][:]))
            c.st.wait(sy)

        exB.close()
        attC = sb("attC", [128, 64 * 256], BF16)
        Qs = [sb(f"Qs{i}", [128, 8 * 256], BF16) for i in range(2)]
        vst = [sb(f"vst{i}", [128, HD], BF16) for i in range(2)]
        oev = [sb(f"oev{i}", [128, HD]) for i in range(2)]
        c.lw.wait(ve)
        c.lw.wait(te)
        lv = [Sem(nc, f"lv{i}") for i in range(2)]
        sgdone = {}
        vcons = {}
        qkdone = {}
        avdone = {}
        odone = {}
        ks = 0
        kv = 0
        for qb in range(NQ):
            qs = Qs[qb % 2]
            if qb >= 2:
                c.tv.wait(ve, qkdone[qb - 2])
            c.st.wait(ve)
            for cc in range(8):
                i = ve.tensor_scalar_mul(
                    qs[:, cc * 256:(cc + 1) * 256],
                    xT[:, cc * L + qb * 256:cc * L + (qb + 1) * 256],
                    qkc[:, cc:cc + 1])
            c.vv.inc(i)
            c.vv.wait(te)
            nk = 2 * qb + 2
            if qb >= 1:
                c.tv.wait(sc, avdone[qb - 1])
            for kt in range(nk):
                ps = psS[ks % 2]
                if ks >= 2:
                    c.sv.wait(te, sgdone[ks - 2])
                for cc in range(8):
                    mm = te.matmul(ps[:],
                                   xT[:, cc * L + kt * 128:cc * L + (kt + 1) * 128],
                                   qs[:, cc * 256:(cc + 1) * 256],
                                   start=(cc == 0), stop=(cc == 7))
                c.tv.inc(mm)
                if kt == nk - 1:
                    qkdone[qb] = c.tv.n
                c.tv.wait(sc)
                sgdone[ks] = c.sv.inc(
                    sc.activation(attC[:, kt * 256:(kt + 1) * 256], ps[:],
                                  AF.Sigmoid, scale=SIG_SCALE))
                ks += 1
            c.sv.wait(ve)
            ve.tensor_mul(attC[:, (nk - 2) * 256:(nk - 1) * 256],
                          attC[:, (nk - 2) * 256:(nk - 1) * 256], m0[:])
            c.vv.inc(ve.tensor_mul(attC[:, (nk - 1) * 256:nk * 256],
                                   attC[:, (nk - 1) * 256:nk * 256], m1[:]))
            c.vv.wait(te)
            po = psO
            if qb >= 1:
                c.vv.wait(te, odone[qb - 1])
            for kt in range(nk):
                vb_ = vst[kv % 2]
                if kv >= 2:
                    c.tv.wait(sy, vcons[kv - 2])
                lv[kv % 2].d(sy.dma_start(vb_[:], vbf[kt * 128:(kt + 1) * 128, :]))
                lv[kv % 2].wait(te)
                for qsub in range(2):
                    for hh in range(2):
                        mm = te.matmul(
                            po[:, (qsub * 2 + hh) * 512:(qsub * 2 + hh + 1) * 512],
                            attC[:, kt * 256 + qsub * 128:kt * 256 + qsub * 128 + 128],
                            vb_[:, hh * 512:(hh + 1) * 512],
                            start=(kt == 0), stop=(kt == nk - 1))
                vcons[kv] = c.tv.inc(mm)
                kv += 1
            avdone[qb] = c.tv.n
            c.tv.wait(ve)
            ov0, ov1 = oev[0], oev[1]
            ve.tensor_mul(ov0[:], po[:, 0:1024], vbrep[:])
            c.vv.inc(ve.tensor_mul(ov1[:], po[:, 1024:2048], vbrep[:]))
            odone[qb] = c.vv.n
            c.vv.wait(sy)
            c.st.d(sy.dma_start(ao_o[qb * 256:qb * 256 + 128, :], ov0[:]))
            c.st.d(sy.dma_start(ao_o[qb * 256 + 128:(qb + 1) * 256, :], ov1[:]))
        c.st.wait(sy)
    return nc


def build_pattn():
    nc = bass.Bass("TRN2", target_bir_lowering=False,
                   detect_race_conditions=os.environ.get("KDET") == "1")
    x2_i = nc.dram_tensor("x2_i", [TPC, D], F32, kind="ExternalInput").ap()
    ao_i = nc.dram_tensor("ao_i", [TPC, D], F32, kind="ExternalInput").ap()
    W1T_ = nc.dram_tensor("W1T_", [D, D], F32, kind="ExternalInput").ap()
    bias_r = nc.dram_tensor("bias_r", [1, D], F32, kind="ExternalInput").ap()
    lA = nc.dram_tensor("lA", [D, 128], F32, kind="ExternalInput").ap()
    lB = nc.dram_tensor("lB", [128, D], F32, kind="ExternalInput").ap()
    lw_c = nc.dram_tensor("lw_c", [128, 1], F32, kind="ExternalInput").ap()
    idb_i = nc.dram_tensor("idb", [128, 128], BF16, kind="ExternalInput").ap()
    x4_o = nc.dram_tensor("x4", [TPC, D], F32, kind="ExternalOutput").ap()

    x3f = nc.dram_tensor("x3f", [TPC, D], F32, kind="Internal").ap()
    x3bf = nc.dram_tensor("x3bf", [TPC, D], BF16, kind="Internal").ap()
    zsp = nc.dram_tensor("zsp2", [16 * NT, 128 * 256], F32, kind="Internal").ap()

    c = Ctx(nc)
    sy, gp, ve, sc, te = c.sy, c.gp, c.ve, c.sc, c.te

    with ExitStack() as ex:
        sb = lambda n_, s_, dt=F32: ex.enter_context(nc.sbuf_tensor(n_, s_, dt))
        uT = sb("uT", [128, TPC], BF16)
        identB = sb("identB", [128, 128], BF16)
        stats = sb("stats", [128, 16])
        epsc = sb("epsc", [128, 1])
        lwc = sb("lwc", [128, 1])

        c.ld.d(sy.dma_start(identB[:], idb_i))
        c.ld.d(sy.dma_start(lwc[:], lw_c))
        ve.memset(epsc[:], EPS_LN)

        exT = ExitStack()
        x3T = exT.enter_context(nc.sbuf_tensor("x3T", [128, 32 * TPC], BF16))

        with ExitStack() as e0:
            s0 = lambda n_, s_, dt=F32: e0.enter_context(nc.sbuf_tensor(n_, s_, dt))
            x2t = s0("x2t", [128, D])
            aot = s0("aot", [128, D])
            x3t = s0("x3t", [128, D])
            x3b = s0("x3b", [128, D], BF16)
            scD = s0("scD0", [128, D])
            l2x = Sem(nc, "l2x")
            for t in range(NT):
                if t > 0:
                    c.vv.wait(sy)
                l2x.d(sy.dma_start(x2t[:], x2_i[t * 128:(t + 1) * 128, :]))
                l2x.d(sy.dma_start(aot[:], ao_i[t * 128:(t + 1) * 128, :]))
                l2x.wait(ve)
                if t > 0:
                    c.st.wait(ve)
                c.vv.inc(ve.tensor_add(x3t[:], x2t[:], aot[:]))
                c.vv.wait(sc)
                emit_ln(c, x3t[:], x3t[:], D, stats[:, 0:8], scD[:],
                        None, None, eps_ap=epsc[:, 0:1])
                c.vv.inc(ve.tensor_copy(x3b[:], x3t[:]))
                c.vv.wait(sy)
                c.st.d(sy.dma_start(x3f[t * 128:(t + 1) * 128, :], x3t[:]))
                c.st.d(sy.dma_start(x3bf[t * 128:(t + 1) * 128, :], x3b[:]))
            c.st.wait(sy)
        ltr = Sem(nc, "ltr")
        for dc in range(32):
            src = AP(x3bf.tensor, x3bf.offset + dc * 128, [[D, TPC], [1, 128]])
            ltr.d(sy.dma_start_transpose(x3T[:, dc * TPC:(dc + 1) * TPC], src))

        with ExitStack() as e1:
            s1 = lambda n_, s_, dt=F32: e1.enter_context(nc.sbuf_tensor(n_, s_, dt))
            Abf = s1("Abf", [128, 32 * 128], BF16)
            ub = s1("ub", [128, 128], BF16)
            psu = e1.enter_context(nc.psum_tensor("psu", [128, 128], F32))
            psb = e1.enter_context(nc.psum_tensor("psb", [128, 128], BF16))
            c.st.wait(gp)
            c.lw.d(gp.dma_start(Abf.ap().rearrange("p (k n) -> p k n", n=128),
                                w_chunks_view(lA, D, 128)))
            c.lw.wait(te)
            ltr.wait(te)
            for t in range(NT):
                for dc in range(32):
                    mm = te.matmul(psu[:], x3T[:, dc * TPC + t * 128:dc * TPC + (t + 1) * 128],
                                   Abf[:, dc * 128:(dc + 1) * 128],
                                   start=(dc == 0), stop=(dc == 31))
                c.tv.inc(mm)
                c.tv.wait(ve)
                c.vv.inc(ve.tensor_copy(ub[:], psu[:]))
                c.vv.wait(te)
                c.tv.inc(te.transpose(psb[:], ub[:], identB[:]))
                c.tv.wait(ve)
                c.vv.inc(ve.tensor_copy(uT[:, t * 128:(t + 1) * 128], psb[:]))
                c.vv.wait(te)

        with ExitStack() as e2:
            s2 = lambda n_, s_, dt=F32: e2.enter_context(nc.sbuf_tensor(n_, s_, dt))
            B2s = s2("B2s", [128, D], BF16)
            Wbuf = [s2(f"Wbuf{i}", [128, 32 * 256], BF16) for i in range(2)]
            zt = [s2(f"zt{i}", [128, 256]) for i in range(2)]
            psz = [e2.enter_context(nc.psum_tensor(f"psz{i}", [128, 256], F32))
                   for i in range(2)]
            c.join([te, ve, sc, sy, gp])
            c.vv.wait(gp)
            c.lw.d(gp.dma_start(B2s[:], lB))
            c.lw.wait(ve)
            c.ld.wait(ve)
            c.vv.inc(ve.tensor_scalar(B2s[:], B2s[:], lwc[:, 0:1], 2.0,
                                      ALU.mult, ALU.mult))
            wtv = {}
            zdone = {}
            for ob in range(16):
                if ob >= 2:
                    c.tv.wait(gp, wtv[ob - 2])
                else:
                    c.vv.wait(gp)
                c.lw.d(gp.dma_start(
                    Wbuf[ob % 2].ap().rearrange("p (k n) -> p k n", n=256),
                    AP(W1T_.tensor, W1T_.offset + ob * 256,
                       [[D, 128], [128 * D, 32], [1, 256]])))
                c.lw.wait(te)
                c.vv.wait(te)
                for t in range(NT):
                    k = ob * NT + t
                    ps = psz[k % 2]
                    if k >= 2:
                        c.vv.wait(te, zdone[k - 2])
                    for dc in range(32):
                        te.matmul(ps[:],
                                  x3T[:, dc * TPC + t * 128:dc * TPC + (t + 1) * 128],
                                  Wbuf[ob % 2][:, dc * 256:(dc + 1) * 256],
                                  start=(dc == 0), stop=False)
                    mm = te.matmul(ps[:], uT[:, t * 128:(t + 1) * 128],
                                   B2s[:, ob * 256:(ob + 1) * 256],
                                   start=False, stop=True)
                    c.tv.inc(mm)
                    if t == NT - 1:
                        wtv[ob] = c.tv.n
                    c.tv.wait(ve)
                    zdone[k] = c.vv.inc(ve.tensor_copy(zt[k % 2][:], ps[:]))
                    c.vv.wait(sy)
                    c.st.d(sy.dma_start(zsp[ob * NT + t, :], zt[k % 2][:]))
            c.st.wait(sy)
        exT.close()

        with ExitStack() as e3:
            s3 = lambda n_, s_, dt=F32: e3.enter_context(nc.sbuf_tensor(n_, s_, dt))
            br = s3("br", [128, D])
            x3t2 = s3("x3t2", [128, D])
            acc = s3("acc", [128, D])
            xr = s3("xr", [128, D])
            scD = s3("scD", [128, D])
            zrd = s3("zrd", [128, D])
            c.join([te, ve, sc, sy, gp])
            c.ld.d(sy.dma_start(br[:], bcast_row(bias_r, D)))
            lz = Sem(nc, "lz")
            lx3 = Sem(nc, "lx3")
            for t in range(NT):
                if t > 0:
                    c.vv.wait(sy)
                lx3.d(sy.dma_start(x3t2[:], x3f[t * 128:(t + 1) * 128, :]))
                lz.d(sy.dma_start(zrd.ap().rearrange("p (a n) -> p a n", n=256),
                                  AP(zsp.tensor, zsp.offset + t * 128 * 256,
                                     [[256, 128], [NT * 128 * 256, 16], [1, 256]])))
                c.ld.wait(ve)
                lx3.wait(ve)
                lz.wait(ve)
                if t > 0:
                    c.st.wait(ve)
                ve.tensor_add(acc[:], x3t2[:], zrd[:])
                c.vv.inc(ve.tensor_add(acc[:], acc[:], br[:]))
                c.vv.wait(sc)
                emit_ln(c, xr[:], acc[:], D, stats[:, 0:8], scD[:],
                        None, None, eps_ap=epsc[:, 0:1])
                c.vv.wait(sy)
                c.st.d(sy.dma_start(x4_o[t * 128:(t + 1) * 128, :], xr[:]))
            c.st.wait(sy)
    return nc


def host_prep_common(inputs):
    pr = {}
    x = np.ascontiguousarray(np.asarray(inputs["L0_attn_x"]).reshape(B * L, D))
    pr["x_rows"] = [np.ascontiguousarray(x[i * TPC:(i + 1) * TPC]) for i in range(NCORES)]
    pr["xT"] = [np.ascontiguousarray(x[i * TPC:(i + 1) * TPC].T) for i in range(NCORES)]
    pr["ident"] = np.eye(128, dtype=np.float32)
    return pr


def run_spmd(nc, in_maps):
    res = run_bass_kernel_spmd(nc, in_maps, core_ids=list(range(NCORES)))
    return res.results


_NC_CACHE = {}


def cached_nc(name, builder):
    if name not in _NC_CACHE:
        _NC_CACHE[name] = builder()
    return _NC_CACHE[name]


def _np(x):
    return np.ascontiguousarray(np.asarray(x))


def kernel(**inputs):
    inp = inputs
    pr = host_prep_common(inp)
    identf = pr["ident"]
    identb = identf.astype(_BF)
    qf = np.arange(256)[None, :]
    sp = np.arange(128)[:, None]
    m0 = (sp <= qf).astype(np.float32).astype(_BF)
    m1 = (sp + 128 <= qf).astype(np.float32).astype(_BF)

    nc = cached_nc("winA0", build_winA0)
    WinT = _np(inp["L0_attn_Win"].T)
    maps = [dict(xT=pr["xT"][i], WinT=WinT,
                 bin_r=_np(inp["L0_attn_bin"][None, :]),
                 l1w_r=_np(inp["L0_attn_ln1_w"][0][None, :]),
                 l1b_r=_np(inp["L0_attn_ln1_b"][0][None, :]))
            for i in range(NCORES)]
    res = run_spmd(nc, maps)
    h = np.concatenate([r["h"] for r in res], 0)
    y = np.concatenate([r["y"] for r in res], 0)

    ncS = cached_nc("sconv", build_sconv)
    for s_ in range(DEPTH):
        yb = _np(y.reshape(B, CPC, 128, DIM))
        yt = _np(np.transpose(yb, (2, 3, 0, 1)))
        mapsS = []
        for i in range(NCORES):
            chs = slice(i * CHPC, (i + 1) * CHPC)
            mapsS.append(dict(
                yT=_np(yt[:, chs].reshape(128, -1)),
                phr=_np(inp["L0_attn_phz_re"][s_][chs][:, None]),
                phi=_np(inp["L0_attn_phz_im"][s_][chs][:, None]),
                lcr=_np(inp["L0_attn_lc_re"][s_][chs][:, None]),
                lci=_np(inp["L0_attn_lc_im"][s_][chs][:, None]),
                idf=identf))
        resS = run_spmd(ncS, mapsS)
        conv = np.zeros((B, L, DIM), np.float32)
        for i in range(NCORES):
            ct = np.asarray(resS[i]["convT"]).reshape(2, CPC, CHPC, 132)
            conv[:, :, i * CHPC:(i + 1) * CHPC] = (
                ct[:, :, :, 0:128].transpose(0, 1, 3, 2).reshape(B, L, CHPC))
        conv = conv.reshape(B * L, DIM)
        last = (s_ == DEPTH - 1)
        ncC = cached_nc(f"scanC{int(last)}", lambda last=last: build_scanC(last))
        mapsC = []
        for i in range(NCORES):
            rows = slice(i * TPC, (i + 1) * TPC)
            m = dict(h_i=_np(h[rows]), conv_i=_np(conv[rows]),
                     scl_r=_np(inp["L0_attn_sc_lin"][s_][None, :]),
                     l2w_r=_np(inp["L0_attn_ln2_w"][s_][None, :]),
                     l2b_r=_np(inp["L0_attn_ln2_b"][s_][None, :]),
                     W1T=_np(inp["L0_attn_W1"][s_].T),
                     b1_c=_np(np.asarray(inp["L0_attn_b1"][s_]).reshape(16, 128).T),
                     W2T=_np(inp["L0_attn_W2"][s_].T),
                     b2_r=_np(inp["L0_attn_b2"][s_][None, :]),
                     idb=identb)
            if last:
                m.update(lnlw_r=_np(inp["L0_attn_lnl_w"][None, :]),
                         lnlb_r=_np(inp["L0_attn_lnl_b"][None, :]),
                         WoutT=_np(inp["L0_attn_Wout"].T),
                         bout_r=_np(inp["L0_attn_bout"][None, :]),
                         x_i=pr["x_rows"][i])
            else:
                m.update(l1w_r=_np(inp["L0_attn_ln1_w"][s_ + 1][None, :]),
                         l1b_r=_np(inp["L0_attn_ln1_b"][s_ + 1][None, :]))
            mapsC.append(m)
        resC = run_spmd(ncC, mapsC)
        if last:
            x1 = np.concatenate([r["x1"] for r in resC], 0)
        else:
            h = np.concatenate([r["h_o"] for r in resC], 0)
            y = np.concatenate([r["y_o"] for r in resC], 0)

    ncZ = cached_nc("l0ffn", build_l0ffn)
    W0T = _np(inp["L0_ffn_W"].T)
    Aall = _np(np.asarray(inp["L0_ffn_A"]).transpose(1, 0, 2).reshape(D, 128))
    Ball = _np(np.asarray(inp["L0_ffn_B"]).reshape(128, D))
    p_c = _np(np.repeat(np.asarray(inp["L0_ffn_p_scores"]), 16)[:, None])
    mapsZ = [dict(x1_i=_np(x1[i * TPC:(i + 1) * TPC]), W0T=W0T,
                  b_r=_np(inp["L0_ffn_b"][None, :]), Aall=Aall, Ball=Ball,
                  p_c=p_c, idb=identb) for i in range(NCORES)]
    resZ = run_spmd(ncZ, mapsZ)
    x2 = np.concatenate([r["x2"] for r in resZ], 0)

    ncT = cached_nc("attn", build_attn)
    x2b = x2.reshape(B, L, D)
    mapsT = []
    for i in range(NCORES):
        b_, h_ = i // NH, i % NH
        xs = _np(x2b[b_][:, h_ * HD:(h_ + 1) * HD])
        mapsT.append(dict(xh=xs, xhT=_np(xs.T),
                          bq_r=_np(inp["L1_attn_bv_q"][h_][None, :]),
                          bk_r=_np(inp["L1_attn_bv_k"][h_][None, :]),
                          bv_r=_np(inp["L1_attn_bv_v"][h_][None, :]),
                          m0=m0, m1=m1, idf=identf))
    resT = run_spmd(ncT, mapsT)
    ao = np.zeros((B, L, D), np.float32)
    for i in range(NCORES):
        b_, h_ = i // NH, i % NH
        ao[b_][:, h_ * HD:(h_ + 1) * HD] = np.asarray(resT[i]["ao"])
    ao = ao.reshape(B * L, D)

    ncP = cached_nc("pattn", build_pattn)
    W1T_ = _np(inp["L1_ffn_W"].T)
    lA = _np(np.asarray(inp["L1_ffn_lora_A"]).transpose(2, 0, 1).reshape(D, 128))
    lB = _np(np.asarray(inp["L1_ffn_lora_B"]).transpose(0, 2, 1).reshape(128, D))
    lw_c = _np(np.repeat(np.asarray(inp["L1_ffn_lora_weights"]), 16)[:, None])
    mapsP = [dict(x2_i=_np(x2[i * TPC:(i + 1) * TPC]),
                  ao_i=_np(ao[i * TPC:(i + 1) * TPC]),
                  W1T_=W1T_, bias_r=_np(inp["L1_ffn_bias"][None, :]),
                  lA=lA, lB=lB, lw_c=lw_c, idb=identb) for i in range(NCORES)]
    resP = run_spmd(ncP, mapsP)
    x4 = np.concatenate([r["x4"] for r in resP], 0)
    return np.ascontiguousarray(x4.reshape(B, L, D))
